# version 04 (final)
# baseline (speedup 1.0000x reference)
import numpy as np
import ml_dtypes
import concourse.bass as bass
import concourse.mybir as mybir
from concourse.bass_utils import run_bass_kernel_spmd
from contextlib import ExitStack

F32 = mybir.dt.float32
BF16 = mybir.dt.bfloat16
I32 = mybir.dt.int32
AF = mybir.ActivationFunctionType
ALU = mybir.AluOpType
AX = mybir.AxisListType

NT = 2048
D = 4096
HID = 11008
NDC = D // 128
NKC = HID // 128
RB = 256
NRB = HID // RB
NH = 32
SCALE = float(192.0 ** -0.5)
NEG = -1.0e30
ENGN = ["sync", "scalar", "vector", "tensor", "gpsimd"]


class Ctr:
    __slots__ = ("sem", "n")

    def __init__(self, sem, n=0):
        self.sem = sem
        self.n = n


class Em:
    def __init__(self, live, eng, C):
        self.live = live
        self.eng = eng
        self.C = C

    def op(self, q, m, *a, inc=None, k=1, **kw):
        c = self.C[inc] if inc else None
        if self.live == q:
            i = getattr(self.eng, m)(*a, **kw)
            if c is not None:
                i.then_inc(c.sem, k)
        if c is not None:
            c.n += k

    def dma(self, q, out, in_, inc, **kw):
        self.op(q, "dma_start", out, in_, inc=inc, k=16, **kw)

    def dmag(self, q, out, in_, base, G, **kw):
        tot = sum(self.C[base + str(i)].n for i in range(G)) // 16
        s = base + str(tot % G)
        self.wait(q, s)
        self.dma(q, out, in_, s, **kw)
        return s, self.C[s].n

    def wait(self, q, ctr, val=None):
        if self.live == q:
            c = self.C[ctr]
            self.eng.wait_ge(c.sem, c.n if val is None else val)

    def bar(self):
        for c in self.C.values():
            if c.n > 0:
                self.eng.wait_ge(c.sem, c.n)

    def settle(self, q, buf, n=2, inc=None):
        for _ in range(n):
            if q == "vector":
                self.op("vector", "tensor_copy", buf[:, 32:64], buf[:, 0:32], inc=inc)
            else:
                self.op("scalar", "copy", buf[:, 32:64], buf[:, 0:32], inc=inc)


def sb_ap(t, offset, dims):
    th = t.tensor if hasattr(t, "tensor") else t
    return bass.AP(th, offset, dims)


_UID = [0]


def nm(base):
    _UID[0] += 1
    return "%s_%d" % (base, _UID[0])


def ph_dequant(em, t, pk_key, s_key, n_key, rows):
    wp, sp, wn = t[pk_key], t[s_key], t[n_key]
    ncp = (D if rows == HID else HID) // 128
    icols = ncp * 64
    wcols = ncp * 128
    pk, sc, ms, lx, dq = t["pk"], t["sc"], t["ms"], t["lx"], t["dq"]
    deq_dve = [0] * (rows // 128)
    deq_wr = [None] * (rows // 128)
    for it in range(rows // 128):
        par = it % 2
        r0 = it * 128
        if it >= 2:
            em.wait("gpsimd", "s_dve", deq_dve[it - 2])
        l1 = em.dmag("gpsimd", pk[par][:, 0:icols], wp[r0:r0 + 128, :], "PK", 2)
        l2 = em.dmag("gpsimd", sc[par][:, 0:2 * ncp], sp[r0:r0 + 128, :], "PK", 2)
        em.wait("vector", l1[0], l1[1])
        em.wait("vector", l2[0], l2[1])
        if it >= 2:
            s_, v_ = deq_wr[it - 2]
            em.wait("vector", s_, v_)
        em.op("vector", "tensor_single_scalar", ms[:, 0:icols], pk[par][:, 0:icols],
              4, ALU.arith_shift_right, inc="s_dve")
        em.op("vector", "tensor_scalar", lx[:, 0:icols], pk[par][:, 0:icols],
              15, 8, ALU.bitwise_and, ALU.bitwise_xor, inc="s_dve")
        em.wait("vector", "s_dve")
        dqw = dq[par].shape[1]
        msw = ms.shape[1]
        scw = sc[par].shape[1]
        out_m = sb_ap(dq[par], 0, [[dqw, 128], [128, ncp], [1, 64]])
        out_l = sb_ap(dq[par], 64, [[dqw, 128], [128, ncp], [1, 64]])
        in_m = sb_ap(ms, 0, [[msw, 128], [64, ncp], [1, 64]])
        in_l = sb_ap(lx, 0, [[msw, 128], [64, ncp], [1, 64]])
        s_m = sb_ap(sc[par], 0, [[scw, 128], [2, ncp], [0, 64]])
        s_l = sb_ap(sc[par], 1, [[scw, 128], [2, ncp], [0, 64]])
        em.op("vector", "scalar_tensor_tensor", out_m, in_m, 0.0, s_m, ALU.add, ALU.mult,
              inc="s_dve")
        em.op("vector", "scalar_tensor_tensor", out_l, in_l, -8.0, s_l, ALU.add, ALU.mult,
              inc="s_dve")
        deq_dve[it] = em.C["s_dve"].n
        em.settle("vector", t["stl"], n=1, inc="s_dve")
        em.wait("gpsimd", "s_dve", em.C["s_dve"].n)
        deq_wr[it] = em.dmag("gpsimd", wn[r0:r0 + 128, 0:wcols], dq[par][:, 0:wcols],
                             "WN", 2)


def ph_stats_y(em, t, xsrc):
    xt, yb, st, junk = t["xt"], t["yb"], t["st"], t["junk"]
    used = [0] * (NT // 128)
    for it in range(NT // 128):
        par = it % 2
        r0 = it * 128
        if it >= 2:
            em.wait("sync", "s_dve", used[it - 2])
        xs, xv = em.dmag("sync", xt[par][:, :], xsrc[r0:r0 + 128, :], "XG", 2)
        em.wait("vector", xs, xv)
        em.op("vector", "reduce_sum", st[:, 0:1], xt[par][:, :], axis=AX.X, inc="s_dve")
        em.op("vector", "tensor_mul", junk[:, :], xt[par][:, :], xt[par][:, :],
              inc="s_dve")
        em.wait("vector", "s_dve")
        em.op("vector", "reduce_sum", st[:, 1:2], junk[:, :], axis=AX.X, inc="s_dve")
        em.wait("vector", "s_dve")
        em.op("vector", "tensor_scalar_mul", st[:, 2:3], st[:, 0:1], -1.0 / D, inc="s_dve")
        em.op("vector", "tensor_scalar_mul", st[:, 3:4], st[:, 1:2], 1.0 / D, inc="s_dve")
        em.wait("vector", "s_dve")
        em.op("vector", "tensor_mul", st[:, 4:5], st[:, 2:3], st[:, 2:3], inc="s_dve")
        em.wait("vector", "s_dve")
        em.op("vector", "tensor_sub", st[:, 5:6], st[:, 3:4], st[:, 4:5], inc="s_dve")
        em.wait("vector", "s_dve")
        em.op("vector", "tensor_scalar_add", st[:, 5:6], st[:, 5:6], 1e-5, inc="s_dve")
        dvev = em.C["s_dve"].n
        em.wait("scalar", "s_dve", dvev)
        em.op("scalar", "activation", st[:, 6:7], st[:, 5:6], AF.Ln, inc="s_act")
        em.wait("scalar", "s_act")
        em.op("scalar", "activation", st[:, 6:7], st[:, 6:7], AF.Exp, scale=-0.5,
              inc="s_act")
        actv = em.C["s_act"].n
        em.wait("vector", "s_act", actv)
        if it >= 2:
            ys_, yv_ = t["y_wr"][it - 2]
            em.wait("vector", ys_, yv_)
        em.op("vector", "tensor_scalar", yb[par][:, :], xt[par][:, :], st[:, 2:3],
              st[:, 6:7], ALU.add, ALU.mult, inc="s_dve")
        used[it] = em.C["s_dve"].n
        em.settle("vector", t["stl"], n=1, inc="s_dve")
        em.wait("gpsimd", "s_dve", em.C["s_dve"].n)
        t["y_wr"][it] = em.dmag("gpsimd", t["y_scr"][r0:r0 + 128, :], yb[par][:, :],
                                "YG", 2)


def load_yT(em, t):
    em.wait("sync", "YG0")
    em.wait("sync", "YG1")
    for dc in range(NDC):
        em.dma("sync", t["yT"][:, dc * NT:(dc + 1) * NT],
               t["y_scr"][0:NT, dc * 128:(dc + 1) * 128], "sy_y", transpose=True)
    return em.C["sy_y"].n


def ph_gemm1(em, t):
    yT, w1s, w3s, hst, silu_t = t["yT"], t["w1s"], t["w3s"], t["hst"], t["silu"]
    psA, psB = t["psA"], t["psB"]
    w1n, w3n, h_scr = t["w1n"], t["w3n"], t["h_scr"]
    ld_done = load_yT(em, t)
    w1_loaded = [None] * NRB
    w3_loaded = [None] * NRB
    sl_done = [0] * (NRB * 2)
    ev_done = [0] * (NRB * 2)
    hw_done = [None] * (NRB * 2)

    def load_w(which, rb):
        s = w1s if which == 1 else w3s
        src = w1n if which == 1 else w3n
        ctr = "sy_w1" if which == 1 else "sy_w3"
        r0 = rb * RB
        for dc in range(NDC):
            em.dma("sync", s[:, dc * RB:(dc + 1) * RB],
                   src[r0:r0 + RB, dc * 128:(dc + 1) * 128], ctr, transpose=True)
        if which == 1:
            w1_loaded[rb] = em.C[ctr].n
        else:
            w3_loaded[rb] = em.C[ctr].n

    load_w(1, 0)
    load_w(3, 0)
    for rb in range(NRB):
        for rh in range(2):
            rt = rb * 2 + rh
            em.wait("tensor", "sy_w1", w1_loaded[rb])
            if rt == 0:
                em.wait("tensor", "sy_y", ld_done)
            if rt > 0:
                em.wait("tensor", "s_act", sl_done[rt - 1])
            for dc in range(NDC):
                for tt in range(4):
                    em.op("tensor", "matmul", psA[tt][:, :],
                          w1s[:, dc * RB + rh * 128: dc * RB + rh * 128 + 128],
                          yT[:, dc * NT + tt * 512: dc * NT + tt * 512 + 512],
                          start=(dc == 0), stop=(dc == NDC - 1),
                          inc=("s_pe" if (dc == NDC - 1 and tt == 3) else None))
            pe_a = em.C["s_pe"].n
            em.wait("tensor", "sy_w3", w3_loaded[rb])
            if rt > 0:
                em.wait("tensor", "s_dve", ev_done[rt - 1])
            for dc in range(NDC):
                for tt in range(4):
                    em.op("tensor", "matmul", psB[tt][:, :],
                          w3s[:, dc * RB + rh * 128: dc * RB + rh * 128 + 128],
                          yT[:, dc * NT + tt * 512: dc * NT + tt * 512 + 512],
                          start=(dc == 0), stop=(dc == NDC - 1),
                          inc=("s_pe" if (dc == NDC - 1 and tt == 3) else None))
            pe_b = em.C["s_pe"].n
            em.wait("scalar", "s_pe", pe_a)
            if rt >= 2:
                em.wait("scalar", "s_dve", ev_done[rt - 1])
            for tt in range(4):
                em.op("scalar", "activation", silu_t[(rt % 2) * 4 + tt][:, :],
                      psA[tt][:, :], AF.Silu, inc="s_act")
            sl_done[rt] = em.C["s_act"].n
            em.wait("vector", "s_pe", pe_b)
            em.settle("vector", t["stl"])
            em.wait("vector", "s_act", sl_done[rt])
            if rt >= 2:
                hs_, hv_ = hw_done[rt - 1]
                em.wait("vector", hs_, hv_)
            for tt in range(4):
                em.op("vector", "tensor_mul", hst[rt % 2][:, tt * 512:(tt + 1) * 512],
                      silu_t[(rt % 2) * 4 + tt][:, :], psB[tt][:, :],
                      inc=("s_dve" if tt == 3 else None))
            ev_done[rt] = em.C["s_dve"].n
            em.settle("vector", t["stl"], n=1, inc="s_dve")
            em.wait("gpsimd", "s_dve", em.C["s_dve"].n)
            hw_done[rt] = em.dmag("gpsimd", h_scr[rt, :, :], hst[rt % 2][:, :], "HG", 2)
            if rh == 1 and rb + 1 < NRB:
                em.wait("sync", "s_pe", pe_a)
                load_w(1, rb + 1)
                em.wait("sync", "s_pe", pe_b)
                load_w(3, rb + 1)


def ph_gemm2(em, t, xsrc, xdst, half):
    hb, w2r, xb, ob, ps = t["hb"], t["w2r"], t["xb"], t["ob"], t["ps8"]
    w2n, h_scr = t["w2n"], t["h_scr"]
    NS = len(w2r)
    slot_use = [0] * NS
    hb_rd = 0
    dve_ep = 0
    si = 0
    for tb in range(2):
        if tb > 0:
            em.wait("sync", "s_pe", hb_rd)
        for kc in range(NKC):
            em.dma("sync", hb[:, kc * 1024:(kc + 1) * 1024],
                   h_scr[kc, :, tb * 1024:(tb + 1) * 1024], "sy_h")
        hld = em.C["sy_h"].n
        for d4 in range(8):
            for kc in range(NKC):
                sl = si % NS
                si += 1
                em.wait("scalar", "s_pe", slot_use[sl])
                ws_, wv_ = em.dmag("scalar", w2r[sl][:, :],
                                   w2n[d4 * 512:(d4 + 1) * 512, kc * 128:(kc + 1) * 128],
                                   "W2", 4, transpose=True)
                em.wait("tensor", ws_, wv_)
                if kc == 0:
                    em.wait("tensor", "sy_h", hld)
                    em.wait("tensor", "s_dve", dve_ep)
                for tt in range(8):
                    em.op("tensor", "matmul", ps[tt][:, :],
                          hb[:, kc * 1024 + tt * 128: kc * 1024 + tt * 128 + 128],
                          w2r[sl][:, :],
                          start=(kc == 0), stop=(kc == NKC - 1),
                          inc=("s_pe" if tt == 7 else None))
                slot_use[sl] = em.C["s_pe"].n
            pev = em.C["s_pe"].n
            hb_rd = pev
            em.wait("vector", "s_pe", pev)
            em.settle("vector", t["stl"])
            for hh in range(2):
                em.wait("sync", "s_dve", dve_ep)
                for i in range(4):
                    em.wait("sync", "SW%d" % i)
                for tt in range(hh * 4, hh * 4 + 4):
                    em.dma("sync", xb[tt % 4][:, :],
                           xsrc[tb * 1024 + tt * 128: tb * 1024 + tt * 128 + 128,
                                d4 * 512:(d4 + 1) * 512], "sy_x")
                xld = em.C["sy_x"].n
                em.wait("vector", "sy_x", xld)
                for i in range(4):
                    em.wait("vector", "SW%d" % i)
                for tt in range(hh * 4, hh * 4 + 4):
                    em.op("vector", "scalar_tensor_tensor", ob[tt % 4][:, :],
                          ps[tt][:, :], half, xb[tt % 4][:, :], ALU.mult, ALU.add,
                          inc="s_dve")
                em.settle("vector", t["stl"], n=2, inc="s_dve")
                dve_ep = em.C["s_dve"].n
                em.wait("scalar", "s_dve", dve_ep)
                for tt in range(hh * 4, hh * 4 + 4):
                    em.dmag("scalar",
                            xdst[tb * 1024 + tt * 128: tb * 1024 + tt * 128 + 128,
                                 d4 * 512:(d4 + 1) * 512], ob[tt % 4][:, :], "SW", 4)


def ph_qkv(em, t):
    yT, ring, vring = t["yT"], t["ring"], t["vring"]
    ps = t["ps8"]
    wq_u, wk, wv = t["wq_u"], t["wk"], t["wv"]
    q_scr, k_scr = t["q_scr"], t["k_scr"]
    NS = len(ring)
    ring_use = [0] * NS
    vring_use = [0] * len(vring)
    si = 0
    ld_done = load_yT(em, t)
    dve_free = {"q": 0, "k": 0}
    wr_free = {"q": None, "k": None}
    for h in range(NH if _QKV_PART[0] != 2 else 0):
        for role, wsrc, scr, ev in (("q", wq_u, q_scr, t["qev"]), ("k", wk, k_scr, t["kev"])):
            banks = ps[0:4] if role == "q" else ps[4:8]
            for dc in range(NDC):
                sl = si % NS
                si += 1
                em.wait("gpsimd", "s_pe", ring_use[sl])
                cs_, cv_ = em.dmag("gpsimd", ring[sl][:, :],
                                   wsrc[dc * 128:(dc + 1) * 128, h * 128:(h + 1) * 128],
                                   "GC", 4)
                em.wait("tensor", cs_, cv_)
                if dc == 0:
                    if h == 0 and role == "q":
                        em.wait("tensor", "sy_y", ld_done)
                    em.wait("tensor", "s_dve", dve_free[role])
                for tt in range(4):
                    em.op("tensor", "matmul", banks[tt][:, :], ring[sl][:, :],
                          yT[:, dc * NT + tt * 512: dc * NT + tt * 512 + 512],
                          start=(dc == 0), stop=(dc == NDC - 1),
                          inc=("s_pe" if tt == 3 else None))
                ring_use[sl] = em.C["s_pe"].n
            pev = em.C["s_pe"].n
            evb = ev[h % 2]
            em.wait("vector", "s_pe", pev)
            em.settle("vector", t["stl"])
            if wr_free[role] is not None:
                em.wait("vector", wr_free[role][0], wr_free[role][1])
            for tt in range(4):
                em.op("vector", "tensor_copy", evb[:, tt * 512:(tt + 1) * 512],
                      banks[tt][:, :], inc=("s_dve" if tt == 3 else None))
            dve_free[role] = em.C["s_dve"].n
            em.settle("vector", t["stl"], n=2, inc="s_dve")
            em.wait("scalar", "s_dve", em.C["s_dve"].n)
            wr_free[role] = em.dmag("scalar", scr[h, :, :], evb[:, :], "SW", 4)
    v_dve_last = em.C["s_dve"].n
    v_wr = [0, 0]
    for gidx in range(8 if _QKV_PART[0] != 1 else 0):
        for th in range(2):
            for dc in range(NDC):
                sl = dc % len(vring)
                em.wait("gpsimd", "s_pe", vring_use[sl])
                cs_, cv_ = em.dmag("gpsimd", vring[sl][:, :],
                                   wv[dc * 128:(dc + 1) * 128, gidx * 512:(gidx + 1) * 512],
                                   "GC", 4)
                em.wait("tensor", cs_, cv_)
                if dc == 0:
                    em.wait("tensor", "s_dve", v_dve_last)
                for tt in range(8):
                    em.op("tensor", "matmul", ps[tt][:, :],
                          yT[:, dc * NT + (th * 8 + tt) * 128:
                             dc * NT + (th * 8 + tt) * 128 + 128],
                          vring[sl][:, :],
                          start=(dc == 0), stop=(dc == NDC - 1),
                          inc=("s_pe" if tt == 7 else None))
                vring_use[sl] = em.C["s_pe"].n
            pev = em.C["s_pe"].n
            vev = t["vev"][th]
            em.wait("vector", "s_pe", pev)
            em.settle("vector", t["stl"])
            if v_wr[th] is not None:
                for i in range(4):
                    em.wait("vector", "SW%d" % i)
            for tt in range(8):
                em.op("vector", "tensor_copy", vev[:, tt * 512:(tt + 1) * 512],
                      ps[tt][:, :], inc=("s_dve" if tt == 7 else None))
            v_dve_last = em.C["s_dve"].n
            em.settle("vector", t["stl"], n=2, inc="s_dve")
            em.wait("scalar", "s_dve", em.C["s_dve"].n)
            for tt in range(8):
                em.dmag("scalar",
                        t["v_scr2"][gidx, th * 1024 + tt * 128: th * 1024 + tt * 128 + 128, :],
                        vev[:, tt * 512:(tt + 1) * 512], "SW", 4)
            v_wr[th] = 1


def ph_flash(em, t):
    qh, kh, vh = t["qh"], t["kh"], t["vh"]
    pex, pn, ptT, at_sb, st = t["pex"], t["pn"], t["ptT"], t["at_sb"], t["st"]
    scA, scB, at_ps = t["scA"], t["scB"], t["at_ps"]
    q_scr, k_scr, v_scr2 = t["q_scr"], t["k_scr"], t["v_scr2"]
    p_scr, at_scr = t["p_scr"], t["at_scr"]
    cmr = t["cmrow"]
    for m in range(4):
        em.dma("sync", cmr[:, m * 512:(m + 1) * 512], t["cmask"][m, :, :], "sy_x")
    mld = em.C["sy_x"].n
    em.wait("vector", "sy_x", mld)
    cmask = [cmr[:, m * 512:(m + 1) * 512] for m in range(4)]

    qkv_ld = [0, 0]
    sc_free = [0, 0]
    pt_rd = [0, 0]
    at_wr = [None, None]
    pn_used = 0
    at_prev = 0
    pe_pv = 0
    flash_pe = [0] * NH
    for h in range(NH):
        par = h % 2
        if h >= 2:
            em.wait("sync", "s_pe", flash_pe[h - 2])
        em.dma("sync", qh[par][:, :], q_scr[h, :, :], "sy_q%d" % par)
        em.dma("sync", kh[par][:, :], k_scr[h, :, :], "sy_q%d" % par)
        gidx = h // 4
        src = sb_ap(v_scr2, gidx * NT * 512 + (h % 4) * 128,
                    [[512, 128], [512 * 128, 16], [1, 128]])
        dst = sb_ap(vh[par], 0, [[2048, 128], [128, 16], [1, 128]])
        em.dma("sync", dst, src, "sy_q%d" % par)
        qkv_ld[par] = em.C["sy_q%d" % par].n
        for ib in range(4):
            L = 512 * (ib + 1)
            for aa in range(4):
                a = ib * 4 + aa
                sc = scA if (ib == 3 or (a % 2 == 0)) else scB
                scpar = 0 if sc is scA else 1
                em.wait("tensor", "sy_q%d" % par, qkv_ld[par])
                em.wait("tensor", "s_act", sc_free[scpar])
                for kb in range(ib + 1):
                    em.op("tensor", "matmul", sc[:, kb * 512:(kb + 1) * 512],
                          qh[par][:, a * 128:(a + 1) * 128],
                          kh[par][:, kb * 512:(kb + 1) * 512],
                          start=True, stop=True,
                          inc=("s_pe" if kb == ib else None))
                pe_sc = em.C["s_pe"].n
                em.wait("vector", "s_pe", pe_sc)
                em.settle("vector", t["stl"])
                em.op("vector", "tensor_tensor", sc[:, ib * 512:(ib + 1) * 512],
                      sc[:, ib * 512:(ib + 1) * 512], cmask[a % 4], ALU.add, inc="s_dve")
                em.wait("vector", "s_dve")
                em.op("vector", "reduce_max", st[:, 8:9], sc[:, 0:L], axis=AX.X,
                      inc="s_dve")
                em.wait("vector", "s_dve")
                em.op("vector", "tensor_scalar_mul", st[:, 9:10], st[:, 8:9], -SCALE,
                      inc="s_dve")
                dv_m = em.C["s_dve"].n
                em.wait("scalar", "s_dve", dv_m)
                em.op("scalar", "activation", pex[:, 0:L], sc[:, 0:L], AF.Exp,
                      bias=st[:, 9:10], scale=SCALE, accum_out=st[:, 10:11], inc="s_act")
                sc_free[scpar] = em.C["s_act"].n
                em.wait("vector", "s_act", sc_free[scpar])
                em.op("vector", "reciprocal", st[:, 11:12], st[:, 10:11], inc="s_dve")
                em.wait("vector", "s_dve")
                em.wait("vector", "g_p", pn_used)
                em.op("vector", "tensor_single_scalar", pn[:, 0:L], pex[:, 0:L],
                      st[:, 11:12], ALU.mult, inc="s_dve")
                dv_n = em.C["s_dve"].n
                em.settle("vector", t["stl"], n=1, inc="s_dve")
                dv_n = em.C["s_dve"].n
                em.wait("gpsimd", "s_dve", dv_n)
                em.wait("gpsimd", "sy_pt", pt_rd[par])
                em.dma("gpsimd", p_scr[par][a * 128:(a + 1) * 128, 0:L], pn[:, 0:L], "g_p")
                pn_used = em.C["g_p"].n
            pw = em.C["g_p"].n
            em.wait("sync", "g_p", pw)
            em.wait("sync", "s_pe", pe_pv)
            for kc in range(4 * (ib + 1)):
                em.dma("sync", ptT[:, kc * 512:(kc + 1) * 512],
                       p_scr[par][ib * 512:(ib + 1) * 512, kc * 128:(kc + 1) * 128],
                       "sy_pt", transpose=True)
            ptld = em.C["sy_pt"].n
            if ib == 3:
                pt_rd[par] = ptld
            em.wait("tensor", "sy_pt", ptld)
            em.wait("tensor", "s_dve", at_prev)
            for kc in range(4 * (ib + 1)):
                em.op("tensor", "matmul", at_ps[:, :], vh[par][:, kc * 128:(kc + 1) * 128],
                      ptT[:, kc * 512:(kc + 1) * 512],
                      start=(kc == 0), stop=(kc == 4 * ib + 3),
                      inc=("s_pe" if kc == 4 * ib + 3 else None))
            pe_pv = em.C["s_pe"].n
            em.wait("vector", "s_pe", pe_pv)
            em.settle("vector", t["stl"])
            if ib == 0 and at_wr[par] is not None:
                em.wait("vector", at_wr[par][0], at_wr[par][1])
            em.op("vector", "tensor_copy", at_sb[par][:, ib * 512:(ib + 1) * 512],
                  at_ps[:, :], inc="s_dve")
            at_prev = em.C["s_dve"].n
        flash_pe[h] = em.C["s_pe"].n
        em.settle("vector", t["stl"], n=1, inc="s_dve")
        at_prev2 = em.C["s_dve"].n
        em.wait("gpsimd", "s_dve", at_prev2)
        at_wr[par] = em.dmag("gpsimd", at_scr[h, :, :], at_sb[par][:, :], "AT", 2)


def ph_wo(em, t, xsrc, xdst):
    atb, woring, xb, ob, ps = t["atb"], t["woring"], t["xb"], t["ob"], t["ps8"]
    wo, at_scr = t["wo"], t["at_scr"]
    NS = len(woring)
    slot_use = [0] * NS
    for kc in range(NH):
        em.dma("sync", atb[:, kc * NT:(kc + 1) * NT], at_scr[kc, :, :], "sy_at")
    atld = em.C["sy_at"].n
    si = 0
    dve_ep = 0
    for tb in range(2):
        for d4 in range(8):
            for kc in range(NDC):
                sl = si % NS
                si += 1
                em.wait("gpsimd", "s_pe", slot_use[sl])
                cs_, cv_ = em.dmag("gpsimd", woring[sl][:, :],
                                   wo[kc * 128:(kc + 1) * 128, d4 * 512:(d4 + 1) * 512],
                                   "GC", 4)
                em.wait("tensor", cs_, cv_)
                if kc == 0:
                    if tb == 0 and d4 == 0:
                        em.wait("tensor", "sy_at", atld)
                    em.wait("tensor", "s_dve", dve_ep)
                for tt in range(8):
                    em.op("tensor", "matmul", ps[tt][:, :],
                          atb[:, kc * NT + (tb * 8 + tt) * 128:
                              kc * NT + (tb * 8 + tt) * 128 + 128],
                          woring[sl][:, :],
                          start=(kc == 0), stop=(kc == NDC - 1),
                          inc=("s_pe" if tt == 7 else None))
                slot_use[sl] = em.C["s_pe"].n
            pev = em.C["s_pe"].n
            em.wait("sync", "s_dve", dve_ep)
            for i in range(4):
                em.wait("sync", "SW%d" % i)
            for tt in range(8):
                em.dma("sync", xb[tt][:, :],
                       xsrc[tb * 1024 + tt * 128: tb * 1024 + tt * 128 + 128,
                            d4 * 512:(d4 + 1) * 512], "sy_x")
            xld = em.C["sy_x"].n
            em.wait("vector", "s_pe", pev)
            em.settle("vector", t["stl"])
            em.wait("vector", "sy_x", xld)
            for i in range(4):
                em.wait("vector", "SW%d" % i)
            for tt in range(8):
                em.op("vector", "tensor_add", ob[tt][:, :], ps[tt][:, :], xb[tt][:, :],
                      inc="s_dve")
            em.settle("vector", t["stl"], n=2, inc="s_dve")
            dve_ep = em.C["s_dve"].n
            em.wait("scalar", "s_dve", dve_ep)
            for tt in range(8):
                em.dmag("scalar", xdst[tb * 1024 + tt * 128: tb * 1024 + tt * 128 + 128,
                                       d4 * 512:(d4 + 1) * 512], ob[tt][:, :], "SW", 4)


def build_nc():
    nc = bass.Bass("TRN2", target_bir_lowering=False)
    g = {}
    g["x_in"] = nc.dram_tensor("x_in", [NT, D], F32, kind="ExternalInput").ap()
    g["wq_u"] = nc.dram_tensor("wq_u", [D, D], F32, kind="ExternalInput").ap()
    g["wk"] = nc.dram_tensor("wk", [D, D], F32, kind="ExternalInput").ap()
    g["wv"] = nc.dram_tensor("wv", [D, D], F32, kind="ExternalInput").ap()
    g["wo"] = nc.dram_tensor("wo", [D, D], F32, kind="ExternalInput").ap()
    g["w1p"] = nc.dram_tensor("w1p", [HID, NDC * 64], I32, kind="ExternalInput").ap()
    g["w3p"] = nc.dram_tensor("w3p", [HID, NDC * 64], I32, kind="ExternalInput").ap()
    g["w2p"] = nc.dram_tensor("w2p", [D, NKC * 64], I32, kind="ExternalInput").ap()
    g["s1"] = nc.dram_tensor("s1", [HID, 64], F32, kind="ExternalInput").ap()
    g["s3"] = nc.dram_tensor("s3", [HID, 64], F32, kind="ExternalInput").ap()
    g["s2"] = nc.dram_tensor("s2", [D, NKC * 2], F32, kind="ExternalInput").ap()
    g["cmask"] = nc.dram_tensor("cmask", [4, 128, 512], BF16, kind="ExternalInput").ap()
    g["out"] = nc.dram_tensor("out", [NT, D], F32, kind="ExternalOutput").ap()
    g["w1n"] = nc.dram_tensor("w1n", [HID, D], BF16).ap()
    g["w3n"] = nc.dram_tensor("w3n", [HID, D], BF16).ap()
    g["w2n"] = nc.dram_tensor("w2n", [D, HID], BF16).ap()
    g["y_scr"] = nc.dram_tensor("y_scr", [NT, D], BF16).ap()
    g["h_scr"] = nc.dram_tensor("h_scr", [NKC, 128, NT], BF16).ap()
    g["q_scr"] = nc.dram_tensor("q_scr", [NH, 128, NT], BF16).ap()
    g["k_scr"] = nc.dram_tensor("k_scr", [NH, 128, NT], BF16).ap()
    g["v_scr2"] = nc.dram_tensor("v_scr2", [8, NT, 512], BF16).ap()
    g["p_scr"] = [nc.dram_tensor("p_scr%d" % i, [NT, NT], BF16).ap() for i in range(2)]
    g["at_scr"] = nc.dram_tensor("at_scr", [NH, 128, NT], BF16).ap()
    g["xr_a"] = nc.dram_tensor("xr_a", [NT, D], F32).ap()
    g["xr_b"] = nc.dram_tensor("xr_b", [NT, D], F32).ap()

    sem_names = (["s_pe", "s_act", "s_dve",
                  "g_p", "sy_x", "sy_y", "sy_w1", "sy_w3", "sy_h", "sy_pt",
                  "sy_at", "sy_q0", "sy_q1"]
                 + ["GC%d" % i for i in range(4)]
                 + ["W2%d" % i for i in range(4)]
                 + ["XG%d" % i for i in range(2)]
                 + ["PK%d" % i for i in range(2)]
                 + ["WN%d" % i for i in range(2)]
                 + ["YG%d" % i for i in range(2)]
                 + ["HG%d" % i for i in range(2)]
                 + ["AT%d" % i for i in range(2)]
                 + ["SW%d" % i for i in range(4)])

    with ExitStack() as stack:
        e = stack.enter_context
        sems = {n: e(nc.semaphore(n)) for n in sem_names}
        g["stl"] = e(nc.sbuf_tensor("stl", [128, 64], F32))
        block = e(nc.Block())
        counts = {n: 0 for n in sem_names}
        shared = {"y_wr": [None] * (NT // 128)}

        def run_phase(alloc_fn, emit_fn):
            nonlocal counts, shared
            with ExitStack() as pstack:
                t = dict(g)
                t.update(shared)
                t.update(alloc_fn(pstack.enter_context))
                final = [None]
                shr = [None]
                for name in ENGN:
                    C = {n: Ctr(sems[n], counts[n]) for n in sem_names}
                    t2 = dict(t)
                    t2["y_wr"] = list(shared["y_wr"])

                    def body(eng, name=name, C=C, t2=t2):
                        em2 = Em(name, eng, C)
                        em2.bar()
                        emit_fn(em2, t2)
                    getattr(block, name)(body)
                    final[0] = {n: C[n].n for n in sem_names}
                    shr[0] = t2["y_wr"]
                counts = final[0]
                shared["y_wr"] = shr[0]

        def alloc_stats(e):
            return {
                "xt": [e(nc.sbuf_tensor(nm("sxt"), [128, D], F32)) for i in range(2)],
                "yb": [e(nc.sbuf_tensor(nm("syb"), [128, D], BF16)) for i in range(2)],
                "st": e(nc.sbuf_tensor(nm("sst"), [128, 16], F32)),
                "junk": e(nc.sbuf_tensor(nm("sjunk"), [128, D], F32)),
            }

        def alloc_deq(e):
            return {
                "pk": [e(nc.sbuf_tensor(nm("pk"), [128, NKC * 64], I32)) for i in range(2)],
                "sc": [e(nc.sbuf_tensor(nm("dsc"), [128, NKC * 2], F32)) for i in range(2)],
                "ms": e(nc.sbuf_tensor(nm("dms"), [128, NKC * 64], I32)),
                "lx": e(nc.sbuf_tensor(nm("dlx"), [128, NKC * 64], I32)),
                "dq": [e(nc.sbuf_tensor(nm("ddq"), [128, HID], BF16)) for i in range(2)],
            }

        def alloc_g1(e):
            return {
                "yT": e(nc.sbuf_tensor(nm("yT"), [128, NDC * NT], BF16, is_transpose_dest=True)),
                "w1s": e(nc.sbuf_tensor(nm("g1w1"), [128, NDC * RB], BF16, is_transpose_dest=True)),
                "w3s": e(nc.sbuf_tensor(nm("g1w3"), [128, NDC * RB], BF16, is_transpose_dest=True)),
                "hst": [e(nc.sbuf_tensor(nm("g1h"), [128, NT], BF16)) for i in range(2)],
                "silu": [e(nc.sbuf_tensor(nm("g1s"), [128, 512], F32)) for i in range(8)],
                "psA": [e(nc.psum_tensor(nm("psA"), [128, 512], F32)) for i in range(4)],
                "psB": [e(nc.psum_tensor(nm("psB"), [128, 512], F32)) for i in range(4)],
            }

        def alloc_g2(e):
            return {
                "hb": e(nc.sbuf_tensor(nm("g2hb"), [128, NKC * 1024], BF16)),
                "w2r": [e(nc.sbuf_tensor(nm("g2w"), [128, 512], BF16,
                                         is_transpose_dest=True)) for i in range(8)],
                "xb": [e(nc.sbuf_tensor(nm("g2x"), [128, 512], F32)) for i in range(4)],
                "ob": [e(nc.sbuf_tensor(nm("g2o"), [128, 512], F32)) for i in range(4)],
                "ps8": [e(nc.psum_tensor(nm("g2p"), [128, 512], F32)) for i in range(8)],
            }

        def alloc_qkv(e):
            return {
                "yT": e(nc.sbuf_tensor(nm("yT2"), [128, NDC * NT], BF16, is_transpose_dest=True)),
                "ring": [e(nc.sbuf_tensor(nm("qr"), [128, 128], BF16)) for i in range(64)],
                "vring": [e(nc.sbuf_tensor(nm("vr"), [128, 512], BF16)) for i in range(4)],
                "qev": [e(nc.sbuf_tensor(nm("qe"), [128, NT], BF16)) for i in range(2)],
                "kev": [e(nc.sbuf_tensor(nm("ke"), [128, NT], BF16)) for i in range(2)],
                "vev": [e(nc.sbuf_tensor(nm("ve"), [128, 4096], BF16)) for i in range(2)],
                "ps8": [e(nc.psum_tensor(nm("qp"), [128, 512], F32)) for i in range(8)],
            }

        def alloc_flash(e):
            return {
                "qh": [e(nc.sbuf_tensor(nm("fq"), [128, NT], BF16)) for i in range(2)],
                "kh": [e(nc.sbuf_tensor(nm("fk"), [128, NT], BF16)) for i in range(2)],
                "vh": [e(nc.sbuf_tensor(nm("fv"), [128, NT], BF16)) for i in range(2)],
                "pex": e(nc.sbuf_tensor(nm("fpex"), [128, NT], F32)),
                "pn": e(nc.sbuf_tensor(nm("fpn"), [128, NT], BF16)),
                "ptT": e(nc.sbuf_tensor(nm("fptT"), [128, 16 * 512], BF16,
                                        is_transpose_dest=True)),
                "at_sb": [e(nc.sbuf_tensor(nm("fat"), [128, NT], BF16)) for i in range(2)],
                "st": e(nc.sbuf_tensor(nm("fst"), [128, 16], F32)),
                "cmrow": e(nc.sbuf_tensor(nm("fcm"), [128, 4 * 512], BF16)),
                "scA": e(nc.psum_tensor(nm("fscA"), [128, 2048], F32)),
                "scB": e(nc.psum_tensor(nm("fscB"), [128, 1536], F32)),
                "at_ps": e(nc.psum_tensor(nm("fatp"), [128, 512], F32)),
            }

        def alloc_wo(e):
            return {
                "atb": e(nc.sbuf_tensor(nm("woat"), [128, NH * NT], BF16)),
                "woring": [e(nc.sbuf_tensor(nm("wor"), [128, 512], BF16))
                           for i in range(4)],
                "xb": [e(nc.sbuf_tensor(nm("wox"), [128, 512], F32)) for i in range(8)],
                "ob": [e(nc.sbuf_tensor(nm("woo"), [128, 512], F32)) for i in range(8)],
                "ps8": [e(nc.psum_tensor(nm("wop"), [128, 512], F32)) for i in range(8)],
            }

        phases = [
            (alloc_stats, lambda em, t: ph_stats_y(em, t, g["x_in"])),
            (alloc_deq, lambda em, t: ph_dequant(em, t, "w1p", "s1", "w1n", HID)),
            (alloc_deq, lambda em, t: ph_dequant(em, t, "w3p", "s3", "w3n", HID)),
            (alloc_deq, lambda em, t: ph_dequant(em, t, "w2p", "s2", "w2n", D)),
            (alloc_g1, ph_gemm1),
            (alloc_g2, lambda em, t: ph_gemm2(em, t, g["x_in"], g["xr_a"], 0.5)),
            (alloc_stats, lambda em, t: ph_stats_y(em, t, g["xr_a"])),
            (alloc_qkv, ph_qkv),
            (alloc_flash, ph_flash),
            (alloc_wo, lambda em, t: ph_wo(em, t, g["xr_a"], g["xr_b"])),
            (alloc_stats, lambda em, t: ph_stats_y(em, t, g["xr_b"])),
            (alloc_g1, ph_gemm1),
            (alloc_g2, lambda em, t: ph_gemm2(em, t, g["xr_b"], g["out"], 0.5)),
        ]
        for af, ef in phases[:_PHASE_LIMIT[0]]:
            run_phase(af, ef)

        if _DEBUG_TAPS:
            taps = []
            for tn in _DEBUG_TAPS:
                src = g[tn]
                if isinstance(src, list):
                    src = src[0]
                sh = list(src.tensor.shape if hasattr(src, "tensor") else src.shape)
                dt_ = src.dtype
                dst = nc.dram_tensor("dbg_" + tn, sh, dt_, kind="ExternalOutput").ap()
                taps.append((dst, src))

            def tap_emit(em, t):
                for dst, src in taps:
                    em.dma("gpsimd", dst, src, "WN0")
                em.wait("gpsimd", "WN0")
            run_phase(lambda e: {}, tap_emit)
        run_phase(lambda e: {}, lambda em, t: None)
    return nc


_DEBUG_TAPS = []
_LAST = [None]


_PHASE_LIMIT = [99]
_QKV_PART = [0]


_NC_CACHE = None


def kernel(L0_attn_hidden_states, L0_attn_wq, L0_attn_wkv, L0_attn_wo,
           L0_ffn_w1, L0_ffn_s1, L0_ffn_w2, L0_ffn_s2, L0_ffn_w3, L0_ffn_s3):
    x = np.ascontiguousarray(np.asarray(L0_attn_hidden_states, dtype=np.float32))
    wq = np.asarray(L0_attn_wq, dtype=np.float32)
    wkv = np.asarray(L0_attn_wkv, dtype=np.float32)
    wo = np.ascontiguousarray(np.asarray(L0_attn_wo, dtype=np.float32))
    wq_u = np.ascontiguousarray(wq.reshape(D, 32, 192)[:, :, :128].reshape(D, D))
    wk = np.ascontiguousarray(wkv[:, :D])
    wv = np.ascontiguousarray(wkv[:, D:2 * D])
    w1p = np.ascontiguousarray(np.asarray(L0_ffn_w1, dtype=np.int32).reshape(HID, NDC * 64))
    w3p = np.ascontiguousarray(np.asarray(L0_ffn_w3, dtype=np.int32).reshape(HID, NDC * 64))
    w2p = np.ascontiguousarray(np.asarray(L0_ffn_w2, dtype=np.int32).reshape(D, NKC * 64))
    s1 = np.ascontiguousarray(np.asarray(L0_ffn_s1, dtype=np.float32).reshape(HID, 64))
    s3 = np.ascontiguousarray(np.asarray(L0_ffn_s3, dtype=np.float32).reshape(HID, 64))
    s2 = np.ascontiguousarray(np.asarray(L0_ffn_s2, dtype=np.float32).reshape(D, NKC * 2))
    ii = np.arange(128)[:, None]
    jj = np.arange(512)[None, :]
    cmask = np.stack([np.where(jj <= ii + 128 * m, 0.0, NEG) for m in range(4)]).astype(
        ml_dtypes.bfloat16)

    global _NC_CACHE
    if _NC_CACHE is None:
        _NC_CACHE = build_nc()
    nc = _NC_CACHE
    shared = {"wq_u": wq_u, "wk": wk, "wv": wv, "wo": wo,
              "w1p": w1p, "w3p": w3p, "w2p": w2p,
              "s1": s1, "s3": s3, "s2": s2, "cmask": cmask}
    in_maps = [dict(shared, x_in=np.ascontiguousarray(x[c])) for c in range(8)]
    res = run_bass_kernel_spmd(nc, in_maps, core_ids=list(range(8)))
    _LAST[0] = res
    return np.stack([res.results[c]["out"] for c in range(8)], axis=0)
